# version 06
# baseline (speedup 1.0000x reference)
import numpy as np
import ml_dtypes
from contextlib import ExitStack

import concourse.bass as bass
import concourse.mybir as mybir
from concourse.bass_utils import run_bass_kernel_spmd

F32 = mybir.dt.float32
BF16 = mybir.dt.bfloat16
AF = mybir.ActivationFunctionType
ALU = mybir.AluOpType

D = 1024
DFF = 4096
KD = D // 128
KF = DFF // 128
NT = 512

ENG_SEM = {"tensor": "pe", "scalar": "sc", "vector": "ve"}
NQSEM = 8


class Slot:
    __slots__ = ("ready", "weng", "wseq", "users", "frozen", "name")

    def __init__(self, name="", users=None, frozen=False):
        self.ready = None
        self.weng = None
        self.wseq = 0
        self.users = list(users) if users else []
        self.frozen = frozen
        self.name = name


def marks_max(slots):
    mx = {}
    for s in slots:
        for m in ([s.ready] if s.ready else []) + s.users:
            if m[1] > mx.get(m[0], 0):
                mx[m[0]] = m[1]
    return [(k, v) for k, v in mx.items()]


class Region:

    def __init__(self, name, marks0=()):
        self.name = name
        self.marks = list(marks0)
        self.cur = []

    def slot(self, name):
        s = Slot(name, users=list(self.marks))
        self.cur.append(s)
        return s

    def turn(self):
        if self.cur:
            self.marks = marks_max(self.cur)
            self.cur = []


class P:
    def __init__(self, nc, ctx):
        self.nc = nc
        names = ["pe", "sc", "ve"] + [f"sy{i}" for i in range(NQSEM)] + \
                [f"wg{i}" for i in range(NQSEM)]
        self.sem = {s: ctx.enter_context(nc.semaphore("s_" + s)) for s in names}
        self.cnt = {s: 0 for s in names}
        ENGS = ("tensor", "scalar", "vector", "sync", "gpsimd")
        self.waited = {e: {s: 0 for s in names} for e in ENGS}
        self.nseq = {e: 0 for e in ENGS}
        self.sealed = {e: 0 for e in ENGS}
        self.qi = {"sync": 0, "gpsimd": 0}
        self.bank_i = 0
        self.banks = None

    def eng(self, name):
        return getattr(self.nc, name)

    def wait(self, ename, dep):
        if dep is None:
            return
        sem, val = dep
        if val <= 0 or self.waited[ename][sem] >= val or ENG_SEM.get(ename) == sem:
            return
        self.eng(ename).wait_ge(self.sem[sem], val)
        self.waited[ename][sem] = val

    def _pre(self, ename, reads, writes):
        own = ENG_SEM.get(ename)
        need_drain = False
        for s in reads:
            if s.ready is None:
                assert s.frozen, f"read of unwritten slot {s.name}"
                continue
            if s.weng == ename and own is not None:
                if s.ready[1] > self.sealed[ename]:
                    need_drain = True
            else:
                self.wait(ename, s.ready)
        for s in writes:
            assert not s.frozen, f"write to frozen slot {s.name}"
            for u in s.users:
                if own is not None and u[0] == own:
                    if u[1] > self.sealed[ename]:
                        need_drain = True
                else:
                    self.wait(ename, u)
            if s.weng is not None and s.ready is not None:
                if s.weng == ename and own is not None:
                    if s.ready[1] > self.sealed[ename]:
                        need_drain = True
                elif s.weng != ename:
                    self.wait(ename, s.ready)
        if need_drain:
            self.eng(ename).drain()
            self.sealed[ename] = self.cnt[own]

    def _post(self, ename, ins, sem, n, reads, writes):
        self.nseq[ename] += 1
        ins.then_inc(self.sem[sem], n)
        self.cnt[sem] += n
        mark = (sem, self.cnt[sem])
        for s in reads:
            if not s.frozen:
                s.users.append(mark)
        for s in writes:
            s.ready = mark
            s.weng = ename
            s.wseq = self.nseq[ename]
            s.users = []
        return mark

    def op(self, ename, emit, reads=(), writes=()):
        self._pre(ename, reads, writes)
        ins = emit(self.eng(ename))
        return self._post(ename, ins, ENG_SEM[ename], 1, reads, writes)

    def dma(self, q, out_ap, in_ap, reads=(), writes=(), **kw):
        pfx = "sy" if q == "sync" else "wg"
        sem = f"{pfx}{self.qi[q] % NQSEM}"
        self.qi[q] += 1
        self.wait(q, (sem, self.cnt[sem]))
        self._pre(q, reads, writes)
        ins = self.eng(q).dma_start(out_ap, in_ap, **kw)
        return self._post(q, ins, sem, 16, reads, writes)

    def mm_group(self, bank, mms, start=True, stop=True):
        ename = "tensor"
        if start:
            for u in bank.users:
                self.wait(ename, u)
        n = len(mms)
        for i, (l, r, rds) in enumerate(mms):
            for s in rds:
                if s.ready is not None:
                    self.wait(ename, s.ready)
            ins = self.nc.tensor.matmul(bank.ap, l, r, start=(start and i == 0),
                                        stop=(stop and i == n - 1))
            self.nseq[ename] += 1
            if i == n - 1:
                ins.then_inc(self.sem["pe"], 1)
                self.cnt["pe"] += 1
        mark = ("pe", self.cnt["pe"])
        if stop:
            bank.ready = mark
            bank.weng = ename
            bank.users = []
        for _, _, rds in mms:
            for s in rds:
                if not s.frozen:
                    s.users.append(mark)
        return mark

    def next_bank(self):
        b = self.banks[self.bank_i % 8]
        self.bank_i += 1
        return b

    def marks_now(self):
        return [(s, v) for s, v in self.cnt.items() if v > 0]


def build_program(T=2048):
    HT = T // 2
    NH = HT // NT
    NCH = T // NT
    nc = bass.Bass("TRN2", target_bir_lowering=False)

    xin = nc.dram_tensor("xin", [D, T], F32, kind="ExternalInput").ap()
    y_d = nc.dram_tensor("y", [D, T], F32, kind="ExternalOutput").ap()
    xdram = nc.dram_tensor("xdram", [D, T], F32).ap()

    wp = {}
    for name, mch, kch in [("p_w1", KF, KD), ("p_w2", KF, KD), ("p_h1", KF, KF),
                           ("p_h2", KF, KF), ("p_w3", KD, KF), ("p_wg", KD, KD),
                           ("p_wc", KD, KD), ("p_aw1", KF, KD), ("p_aw3", KF, KD),
                           ("p_aw2", KD, KF)]:
        wp[name] = nc.dram_tensor(name, [mch, 128, kch * 128], F32,
                                  kind="ExternalInput").ap()
    vin = {}
    for name, chk in [("v_bg", KD), ("v_bc", KD), ("v_n1", KD), ("v_n2", KD),
                      ("v_s1", KF), ("v_s2", KF), ("v_t1", KF), ("v_t2", KF),
                      ("v_s3", KD)]:
        vin[name] = nc.dram_tensor(name, [chk, 128], F32, kind="ExternalInput").ap()

    def x_kview(xap, t0, tl):
        return xap.rearrange("(k p) t -> p k t", p=128)[:, :, t0:t0 + tl]

    with ExitStack() as ctx:
        p = P(nc, ctx)
        sb = lambda name, shape, dt: ctx.enter_context(nc.sbuf_tensor(name, shape, dt))

        ones_bf = sb("ones_bf", [128, 128], BF16)
        consts = sb("consts", [128, 4], F32)
        t_bg = sb("t_bg", [128, KD], F32)
        t_bcn = sb("t_bcn", [128, KD], F32)
        t_n1 = sb("t_n1", [128, KD], F32)
        t_n2 = sb("t_n2", [128, KD], F32)
        t_s1 = sb("t_s1", [128, KF], F32)
        t_s2 = sb("t_s2", [128, KF], F32)
        t_t1 = sb("t_t1", [128, KF], F32)
        t_t2 = sb("t_t2", [128, KF], F32)
        t_s3h = sb("t_s3h", [128, KD], F32)
        xb_t = [sb(f"xb{i}", [128, KD * NT], BF16) for i in range(2)]
        xsq_t = [sb(f"xsq{i}", [128, NT], BF16) for i in range(2)]
        tf_t = [sb(f"tf{i}", [128, NT], F32) for i in range(4)]
        rs_t = [sb(f"rs{i}", [128, NT], F32) for i in range(2)]
        mr_t = [sb(f"mr{i}", [128, NT], F32) for i in range(2)]
        xt_t = [sb(f"xt{i}", [128, NT], F32) for i in range(2)]
        pb_t = [ctx.enter_context(nc.psum_tensor(f"pb{i}", [128, NT], F32))
                for i in range(8)]

        EPS_LN = consts[:, 0:1]
        EPS_RMS = consts[:, 1:2]
        NEG_HALF = consts[:, 2:3]
        NEG_ONE = consts[:, 3:4]

        class BankSlot(Slot):
            __slots__ = ("ap",)
        banks = []
        for i in range(8):
            s = BankSlot(f"pb{i}")
            s.ap = pb_t[i][:, :]
            banks.append(s)
        p.banks = banks

        sl_ones = Slot("ones")
        sl_consts = Slot("consts")
        sl_vec = {n: Slot(n) for n in
                  ("bg", "bcn", "n1", "n2", "s1", "s2", "t1", "t2", "s3h")}
        sl_xb = [Slot(f"xb{i}") for i in range(2)]
        sl_xsq = [Slot(f"xsq{i}") for i in range(2)]
        sl_tf = [Slot(f"tf{i}") for i in range(4)]
        sl_rs = [Slot(f"rs{i}") for i in range(2)]
        sl_mr = [Slot(f"mr{i}") for i in range(2)]
        sl_xt = [Slot(f"xt{i}") for i in range(2)]
        sl_xdram = [[Slot(f"xd{r}_{t}") for t in range(NCH)] for r in range(KD)]
        sl_y = [[Slot(f"yy{r}_{t}") for t in range(NCH)] for r in range(KD)]

        block = ctx.enter_context(nc.Block())

        p.op("vector", lambda e: e.memset(ones_bf[:, :], 1.0 / D), writes=[sl_ones])
        p.op("vector", lambda e: e.memset(consts[:, 0:1], 1e-5), writes=[sl_consts])
        p.op("vector", lambda e: e.memset(consts[:, 1:2], 1e-6), writes=[sl_consts])
        p.op("vector", lambda e: e.memset(consts[:, 2:3], -0.5), writes=[sl_consts])
        p.op("vector", lambda e: e.memset(consts[:, 3:4], -1.0), writes=[sl_consts])
        sl_consts.frozen = True
        sl_ones.frozen = True
        with nc.allow_non_contiguous_dma(reason="tiny vector loads"):
            for nm, tile, slot in [("v_bg", t_bg, sl_vec["bg"]), ("v_bc", t_bcn, sl_vec["bcn"]),
                                   ("v_n1", t_n1, sl_vec["n1"]), ("v_n2", t_n2, sl_vec["n2"]),
                                   ("v_s1", t_s1, sl_vec["s1"]), ("v_s2", t_s2, sl_vec["s2"]),
                                   ("v_t1", t_t1, sl_vec["t1"]), ("v_t2", t_t2, sl_vec["t2"]),
                                   ("v_s3", t_s3h, sl_vec["s3h"])]:
                p.dma("sync", tile[:, :], vin[nm].transpose([1, 0]), writes=[slot])
        p.op("vector", lambda e: e.tensor_scalar_mul(t_bcn[:, :], t_bcn[:, :], -1.0),
             reads=[sl_vec["bcn"]], writes=[sl_vec["bcn"]])
        p.op("vector", lambda e: e.tensor_scalar_mul(t_s3h[:, :], t_s3h[:, :], 0.5),
             reads=[sl_vec["s3h"]], writes=[sl_vec["s3h"]])
        for s in sl_vec.values():
            s.frozen = True

        def load_xb(src_ap, src_slots, t0, bi):
            dst = xb_t[bi][:, :].rearrange("p (k t) -> p k t", t=NT)
            p.dma("gpsimd", dst, x_kview(src_ap, t0, NT),
                  reads=src_slots, writes=[sl_xb[bi]])
            return sl_xb[bi], xb_t[bi]

        def stats(xbslot, xb_tile, kind, si, sq_src=None, f1_out=None):
            xb_ap = lambda k: xb_tile[:, k * NT:(k + 1) * NT]
            sqs, sq_ap = (xbslot, xb_ap) if sq_src is None else sq_src
            bank_m = None
            if kind == "ln":
                bank_m = p.next_bank()
                p.mm_group(bank_m, [(ones_bf[:, :], xb_ap(k), (xbslot, sl_ones))
                                    for k in range(KD)])
            bank_q = p.next_bank()
            for batch in range(KD // 2):
                mms = []
                for j in range(2):
                    k = batch * 2 + j
                    p.op("scalar", lambda e, k=k, j=j:
                         e.activation(xsq_t[j][:, :], sq_ap(k), AF.Square),
                         reads=[sqs], writes=[sl_xsq[j]])
                    mms.append((ones_bf[:, :], xsq_t[j][:, :], (sl_xsq[j], sl_ones)))
                p.mm_group(bank_q, mms, start=(batch == 0), stop=(batch == KD // 2 - 1))
            r_sl, r_ap = sl_rs[si], rs_t[si][:, :]
            if kind == "ln":
                mr_sl, mr_ap = sl_mr[si], mr_t[si][:, :]
                p.op("scalar", lambda e: e.activation(mr_ap, bank_m.ap, AF.Square),
                     reads=[bank_m], writes=[mr_sl])
                p.op("vector", lambda e: e.tensor_sub(mr_ap, bank_q.ap, mr_ap),
                     reads=[bank_q, mr_sl], writes=[mr_sl])
                p.op("scalar", lambda e: e.activation(r_ap, mr_ap, AF.Ln, bias=EPS_LN),
                     reads=[mr_sl, sl_consts], writes=[r_sl])
                p.op("scalar", lambda e: e.activation(r_ap, r_ap, AF.Exp, scale=NEG_HALF),
                     reads=[r_sl, sl_consts], writes=[r_sl])
                if f1_out is not None:
                    f1sl, f1ap = f1_out
                    p.op("vector", lambda e: e.tensor_mul(f1ap, mr_ap, r_ap),
                         reads=[mr_sl, r_sl], writes=[f1sl])
                    p.op("vector", lambda e: e.tensor_mul(f1ap, f1ap, r_ap),
                         reads=[f1sl, r_sl], writes=[f1sl])
                    p.op("scalar", lambda e: e.activation(f1ap, f1ap, AF.Ln, bias=EPS_RMS),
                         reads=[f1sl, sl_consts], writes=[f1sl])
                    p.op("scalar", lambda e: e.activation(f1ap, f1ap, AF.Exp, scale=NEG_HALF),
                         reads=[f1sl, sl_consts], writes=[f1sl])
                return r_sl, r_ap, mr_sl, bank_m
            else:
                p.op("scalar", lambda e: e.activation(r_ap, bank_q.ap, AF.Ln, bias=EPS_RMS),
                     reads=[bank_q, sl_consts], writes=[r_sl])
                p.op("scalar", lambda e: e.activation(r_ap, r_ap, AF.Exp, scale=NEG_HALF),
                     reads=[r_sl, sl_consts], writes=[r_sl])
                return r_sl, r_ap, None, None

        def ln_apply(in_f, r_sl, r_ap, bank_m, out_f):
            for h4 in range(2):
                ks = range(h4 * 4, h4 * 4 + 4)
                for k in ks:
                    iap, irds = in_f(k)
                    p.op("vector", lambda e, k=k, iap=iap:
                         e.tensor_sub(tf_t[k % 4][:, :], iap, bank_m.ap),
                         reads=list(irds) + [bank_m], writes=[sl_tf[k % 4]])
                for k in ks:
                    oap, osl = out_f(k)
                    p.op("vector", lambda e, k=k, oap=oap:
                         e.tensor_mul(oap, tf_t[k % 4][:, :], r_ap),
                         reads=[sl_tf[k % 4], r_sl], writes=[osl])

        def emit_ffn(src_ap, src_slots2d, dst_ap, dst_slots2d, A_t, reg_A, zf_t, reg_zf,
                     pool_t, reg_pool):
            zf_ap = lambda k, t: zf_t[:, k * HT + t * NT: k * HT + t * NT + NT]
            zf_state = {}

            def do_s1(half):
                th0 = half * HT
                reg_zf.turn()
                sl_zf = [[reg_zf.slot(f"zf{half}_{k}_{t}") for t in range(NH)]
                         for k in range(KD)]
                for tt in range(NH):
                    t0 = th0 + tt * NT
                    ch = t0 // NT
                    xbs, xbt = load_xb(src_ap, [src_slots2d[r][ch] for r in range(KD)],
                                       t0, tt % 2)
                    r_sl, r_ap, mr_sl, bank_m = stats(xbs, xbt, "ln", tt % 2)
                    ln_apply(lambda k: (xbt[:, k * NT:(k + 1) * NT], (xbs,)),
                             r_sl, r_ap, bank_m,
                             lambda k: (zf_ap(k, tt), sl_zf[k][tt]))
                zf_state[half] = sl_zf

            do_s1(0)
            for half in range(2):
                th0 = half * HT
                u_i, h1_i = half % 2, (half + 1) % 2
                sl_zf = zf_state[half]

                reg_A[u_i].turn()
                sl_u = [[reg_A[u_i].slot(f"u{half}_{m}_{t}") for t in range(NH)]
                        for m in range(KF)]
                u_ap = lambda m, t: A_t[u_i][:, m * HT + t * NT: m * HT + t * NT + NT]
                for rp in reg_pool[:2]:
                    rp.turn()
                for m in range(KF):
                    pi = m % 2
                    pt = pool_t[pi]
                    if m >= 2:
                        reg_pool[pi].turn()
                    w1sl = reg_pool[pi].slot(f"w1_{half}_{m}")
                    w2sl = reg_pool[pi].slot(f"w2_{half}_{m}")
                    p.dma("gpsimd", pt[:, 0:KD * 128], wp["p_w1"][m], writes=[w1sl])
                    p.dma("gpsimd", pt[:, KD * 128:2 * KD * 128], wp["p_w2"][m], writes=[w2sl])
                    for t in range(NH):
                        b1 = p.next_bank()
                        b2 = p.next_bank()
                        p.mm_group(b1, [(pt[:, k * 128:(k + 1) * 128], zf_ap(k, t),
                                         (w1sl, sl_zf[k][t])) for k in range(KD)])
                        p.mm_group(b2, [(pt[:, KD * 128 + k * 128:KD * 128 + (k + 1) * 128],
                                         zf_ap(k, t), (w2sl, sl_zf[k][t])) for k in range(KD)])
                        ti = (m * NH + t) % 2
                        p.op("scalar", lambda e, b1=b1, m=m, ti=ti:
                             e.activation(tf_t[ti][:, :], b1.ap, AF.Silu, scale=t_s1[:, m:m + 1]),
                             reads=[b1, sl_vec["s1"]], writes=[sl_tf[ti]])
                        p.op("vector", lambda e, b2=b2, m=m, t=t, ti=ti:
                             e.scalar_tensor_tensor(u_ap(m, t), b2.ap, t_s2[:, m:m + 1],
                                                    tf_t[ti][:, :], ALU.mult, ALU.mult),
                             reads=[b2, sl_tf[ti], sl_vec["s2"]], writes=[sl_u[m][t]])

                def hstage(wname, tvec_t, tvec_sl, in_ap, in_sl, out_i, tag):
                    reg_A[out_i].turn()
                    o_sl = [[reg_A[out_i].slot(f"{tag}_{m}_{t}") for t in range(NH)]
                            for m in range(KF)]
                    o_ap = lambda m, t: A_t[out_i][:, m * HT + t * NT: m * HT + t * NT + NT]
                    for m in range(KF):
                        pi = m % 3
                        reg_pool[pi].turn()
                        wsl = reg_pool[pi].slot(f"{tag}w_{m}")
                        p.dma("gpsimd", pool_t[pi][:, :], wp[wname][m], writes=[wsl])
                        for t in range(NH):
                            b = p.next_bank()
                            p.mm_group(b, [(pool_t[pi][:, k * 128:(k + 1) * 128],
                                            in_ap(k, t), (wsl, in_sl[k][t]))
                                           for k in range(KF)])
                            p.op("scalar", lambda e, b=b, m=m, t=t:
                                 e.activation(o_ap(m, t), b.ap, AF.Gelu,
                                              scale=tvec_t[:, m:m + 1]),
                                 reads=[b, tvec_sl], writes=[o_sl[m][t]])
                    return o_sl, o_ap

                sl_h1, h1_ap = hstage("p_h1", t_t1, sl_vec["t1"], u_ap, sl_u, h1_i, f"h1_{half}")
                if half == 0:
                    do_s1(1)
                sl_h2, h2_ap = hstage("p_h2", t_t2, sl_vec["t2"], h1_ap, sl_h1, u_i, f"h2_{half}")

                for mo in range(KD):
                    pi = mo % 3
                    reg_pool[pi].turn()
                    wsl = reg_pool[pi].slot(f"w3_{mo}")
                    p.dma("gpsimd", pool_t[pi][:, :], wp["p_w3"][mo], writes=[wsl])
                    for t in range(NH):
                        ch = (th0 + t * NT) // NT
                        b = p.next_bank()
                        p.mm_group(b, [(pool_t[pi][:, k * 128:(k + 1) * 128],
                                        h2_ap(k, t), (wsl, sl_h2[k][t]))
                                       for k in range(KF)])
                        xi_ = (mo * NH + t) % 2
                        xs = sl_xt[xi_]
                        p.dma("sync", xt_t[xi_][:, :],
                              src_ap[mo * 128:(mo + 1) * 128, th0 + t * NT: th0 + (t + 1) * NT],
                              reads=[src_slots2d[mo][ch]], writes=[xs])
                        p.op("vector", lambda e, b=b, mo=mo, xi_=xi_:
                             e.scalar_tensor_tensor(xt_t[xi_][:, :], b.ap, t_s3h[:, mo:mo + 1],
                                                    xt_t[xi_][:, :], ALU.mult, ALU.add),
                             reads=[b, xs, sl_vec["s3h"]], writes=[xs])
                        p.dma("sync",
                              dst_ap[mo * 128:(mo + 1) * 128, th0 + t * NT: th0 + (t + 1) * NT],
                              xt_t[xi_][:, :], reads=[xs], writes=[dst_slots2d[mo][ch]])

        def emit_attn_A(src_ap, src_slots2d, z2_t, sl_z2, z2_ap, wgwc_t, reg_wgwc,
                        xi_t, reg_xi, z1_t, reg_z1, g_t, reg_g, b_t, reg_b,
                        h_t, reg_h):
            sl_wg = [reg_wgwc.slot(f"wg{m}") for m in range(KD)]
            sl_wc = [reg_wgwc.slot(f"wc{m}") for m in range(KD)]
            wg_ap = lambda m, k: wgwc_t[:, m * 1024 + k * 128: m * 1024 + (k + 1) * 128]
            wc_ap = lambda m, k: wgwc_t[:, 8192 + m * 1024 + k * 128: 8192 + m * 1024 + (k + 1) * 128]
            dstg = wgwc_t[:, 0:8192].rearrange("p (m c) -> p m c", c=1024)
            dstc = wgwc_t[:, 8192:16384].rearrange("p (m c) -> p m c", c=1024)
            p.dma("gpsimd", dstg, wp["p_wg"].transpose([1, 0, 2]), writes=sl_wg)
            p.dma("gpsimd", dstc, wp["p_wc"].transpose([1, 0, 2]), writes=sl_wc)
            sl_g = sl_b = sl_h = None
            for c in range(NCH):
                t0 = c * NT
                bi = c % 2
                reg_xi[bi].turn()
                xisl = reg_xi[bi].slot(f"xi{c}")
                src_sl = [src_slots2d[r][c] for r in range(KD)]
                p.dma("sync", xi_t[bi][:, :].rearrange("p (k t) -> p k t", t=NT),
                      x_kview(src_ap, t0, NT), reads=src_sl, writes=[xisl])
                xbs, xbt = load_xb(src_ap, src_sl, t0, bi)
                xi_ap = lambda k, bi=bi: xi_t[bi][:, k * NT:(k + 1) * NT]
                r_sl, r_ap, mr_sl, bank_m = stats(
                    xbs, xbt, "ln", bi, f1_out=(sl_mr[bi], mr_t[bi][:, :]))
                ln_apply(lambda k: (xi_ap(k), (xisl,)), r_sl, r_ap, bank_m,
                         lambda k: (xi_ap(k), xisl))
                reg_z1[bi].turn()
                z1sl = reg_z1[bi].slot(f"z1_{c}")
                z1_ap = lambda k, bi=bi: z1_t[bi][:, k * NT:(k + 1) * NT]
                for k in range(KD):
                    p.op("vector", lambda e, k=k, bi=bi:
                         e.scalar_tensor_tensor(z1_ap(k), xi_ap(k), t_n1[:, k:k + 1],
                                                mr_t[bi][:, :], ALU.mult, ALU.mult),
                         reads=[xisl, sl_mr[bi], sl_vec["n1"]], writes=[z1sl])
                prev_h = sl_h
                reg_g.turn(); reg_b.turn()
                sl_g = [reg_g.slot(f"g{c}_{m}") for m in range(KD)]
                sl_b = [reg_b.slot(f"b{c}_{m}") for m in range(KD)]
                for m in range(KD):
                    bg_ = p.next_bank()
                    bc_ = p.next_bank()
                    p.mm_group(bg_, [(wg_ap(m, k), z1_ap(k), (sl_wg[m], z1sl))
                                     for k in range(KD)])
                    p.mm_group(bc_, [(wc_ap(m, k), z1_ap(k), (sl_wc[m], z1sl))
                                     for k in range(KD)])
                    p.op("scalar", lambda e, bg_=bg_, m=m:
                         e.activation(g_t[:, m * NT:(m + 1) * NT], bg_.ap, AF.Sigmoid,
                                      bias=t_bg[:, m:m + 1]),
                         reads=[bg_, sl_vec["bg"]], writes=[sl_g[m]])
                    p.op("scalar", lambda e, bc_=bc_, m=m:
                         e.activation(b_t[:, m * NT:(m + 1) * NT], bc_.ap, AF.Tanh,
                                      scale=NEG_ONE, bias=t_bcn[:, m:m + 1]),
                         reads=[bc_, sl_vec["bcn"], sl_consts], writes=[sl_b[m]])
                    p.op("vector", lambda e, m=m:
                         e.scalar_tensor_tensor(b_t[:, m * NT:(m + 1) * NT],
                                                g_t[:, m * NT:(m + 1) * NT], 1.0,
                                                b_t[:, m * NT:(m + 1) * NT],
                                                ALU.subtract, ALU.mult),
                         reads=[sl_g[m], sl_b[m]], writes=[sl_b[m]])
                reg_h.turn()
                new_h = [reg_h.slot(f"h{c}_{f}") for f in range(KD)]
                for f in range(KD):
                    if c == 0:
                        init = 0.0
                        rds = [sl_g[f], sl_b[f]]
                    else:
                        init = h_t[:, f * NT + NT - 1: f * NT + NT]
                        rds = [sl_g[f], sl_b[f], prev_h[f]]
                    p.op("vector", lambda e, f=f, init=init:
                         e.tensor_tensor_scan(h_t[:, f * NT:(f + 1) * NT],
                                              g_t[:, f * NT:(f + 1) * NT],
                                              b_t[:, f * NT:(f + 1) * NT],
                                              init, ALU.mult, ALU.add),
                         reads=rds, writes=[new_h[f]])
                sl_h = new_h
                p.op("vector", lambda e, bi=bi: e.tensor_add(xi_t[bi][:, :], xi_t[bi][:, :],
                                                             h_t[:, :]),
                     reads=[xisl] + sl_h, writes=[xisl])
                r2_sl, r2_ap, _, _ = stats(None, xb_t[bi], "rms", bi,
                                           sq_src=(xisl, xi_ap))
                for k in range(KD):
                    p.op("vector", lambda e, k=k, c=c:
                         e.scalar_tensor_tensor(z2_ap(k, c), xi_ap(k), t_n2[:, k:k + 1],
                                                r2_ap, ALU.mult, ALU.mult),
                         reads=[xisl, r2_sl, sl_vec["n2"]], writes=[sl_z2[k][c]])
                p.dma("gpsimd", x_kview(xdram, t0, NT),
                      xi_t[bi][:, :].rearrange("p (k t) -> p k t", t=NT),
                      reads=[xisl] + [sl_xdram[r][c] for r in range(KD)],
                      writes=[sl_xdram[r][c] for r in range(KD)],
                      accum_op=ALU.add)

        def emit_attn_B(sl_z2, z2_ap, uh_t, reg_uh, pool_t, reg_pool):
            for hb in range(2):
                th0 = hb * HT
                reg_uh.turn()
                sl_uh = [[reg_uh.slot(f"uh{hb}_{m}_{t}") for t in range(NH)]
                         for m in range(KF)]
                uh_ap = lambda m, t: uh_t[:, m * HT + t * NT: m * HT + t * NT + NT]
                for rp in reg_pool[:2]:
                    rp.turn()
                for m in range(KF):
                    pi = m % 2
                    pt = pool_t[pi]
                    if m >= 2:
                        reg_pool[pi].turn()
                    w1sl = reg_pool[pi].slot(f"aw1_{hb}_{m}")
                    w3sl = reg_pool[pi].slot(f"aw3_{hb}_{m}")
                    p.dma("gpsimd", pt[:, 0:KD * 128], wp["p_aw1"][m], writes=[w1sl])
                    p.dma("gpsimd", pt[:, KD * 128:2 * KD * 128], wp["p_aw3"][m], writes=[w3sl])
                    for t in range(NH):
                        c = hb * NH + t
                        b1 = p.next_bank()
                        b3 = p.next_bank()
                        p.mm_group(b1, [(pt[:, k * 128:(k + 1) * 128], z2_ap(k, c),
                                         (w1sl, sl_z2[k][c])) for k in range(KD)])
                        p.mm_group(b3, [(pt[:, KD * 128 + k * 128:KD * 128 + (k + 1) * 128],
                                         z2_ap(k, c), (w3sl, sl_z2[k][c])) for k in range(KD)])
                        ti = (m * NH + t) % 2
                        p.op("scalar", lambda e, b1=b1, ti=ti:
                             e.activation(tf_t[ti][:, :], b1.ap, AF.Silu),
                             reads=[b1], writes=[sl_tf[ti]])
                        p.op("vector", lambda e, b3=b3, m=m, t=t, ti=ti:
                             e.tensor_mul(uh_ap(m, t), tf_t[ti][:, :], b3.ap),
                             reads=[b3, sl_tf[ti]], writes=[sl_uh[m][t]])
                for mo in range(KD):
                    pi = mo % 3
                    reg_pool[pi].turn()
                    wsl = reg_pool[pi].slot(f"aw2_{mo}")
                    p.dma("gpsimd", pool_t[pi][:, :], wp["p_aw2"][mo], writes=[wsl])
                    for t in range(NH):
                        c = hb * NH + t
                        b = p.next_bank()
                        p.mm_group(b, [(pool_t[pi][:, k * 128:(k + 1) * 128],
                                        uh_ap(k, t), (wsl, sl_uh[k][t]))
                                       for k in range(KF)])
                        xi_ = (mo * NH + t) % 2
                        xs = sl_xt[xi_]
                        p.dma("sync", xt_t[xi_][:, :],
                              xdram[mo * 128:(mo + 1) * 128, th0 + t * NT: th0 + (t + 1) * NT],
                              reads=[sl_xdram[mo][c]], writes=[xs])
                        p.op("vector", lambda e, b=b, xi_=xi_:
                             e.tensor_add(xt_t[xi_][:, :], xt_t[xi_][:, :], b.ap),
                             reads=[b, xs], writes=[xs])
                        p.dma("sync",
                              xdram[mo * 128:(mo + 1) * 128, th0 + t * NT: th0 + (t + 1) * NT],
                              xt_t[xi_][:, :], reads=[xs], writes=[sl_xdram[mo][c]])

        xin_slots = [[Slot(f"xin{r}_{t}", frozen=True) for t in range(NCH)]
                     for r in range(KD)]

        with nc.sbuf_tensor("A0", [128, KF * HT], BF16) as A0_t, \
             nc.sbuf_tensor("A1", [128, KF * HT], BF16) as A1_t, \
             nc.sbuf_tensor("zf", [128, KD * HT], BF16) as zf_t, \
             nc.sbuf_tensor("poolA0", [128, KF * 128], BF16) as pA0, \
             nc.sbuf_tensor("poolA1", [128, KF * 128], BF16) as pA1, \
             nc.sbuf_tensor("poolA2", [128, KF * 128], BF16) as pA2:
            reg_A = [Region("A0"), Region("A1")]
            reg_zf = Region("zf")
            emit_ffn(xin, xin_slots, xdram, sl_xdram, [A0_t, A1_t], reg_A, zf_t, reg_zf,
                     [pA0, pA1, pA2], [Region(f"poolA{i}") for i in range(3)])
            m_ffn1 = p.marks_now()

        with nc.sbuf_tensor("z2", [128, KD * T], BF16) as z2_t:
            reg_z2 = Region("z2", m_ffn1)
            sl_z2 = [[reg_z2.slot(f"z2_{k}_{c}") for c in range(NCH)] for k in range(KD)]
            z2_ap = lambda k, c: z2_t[:, k * T + c * NT: k * T + c * NT + NT]
            with nc.sbuf_tensor("wgwc", [128, 2 * KD * 1024], BF16) as wgwc_t, \
                 nc.sbuf_tensor("xiA", [128, KD * NT], F32) as xiA_t, \
                 nc.sbuf_tensor("xiB", [128, KD * NT], F32) as xiB_t, \
                 nc.sbuf_tensor("z1A", [128, KD * NT], BF16) as z1A_t, \
                 nc.sbuf_tensor("z1B", [128, KD * NT], BF16) as z1B_t, \
                 nc.sbuf_tensor("gB", [128, KD * NT], F32) as g_t, \
                 nc.sbuf_tensor("bB", [128, KD * NT], F32) as b_t, \
                 nc.sbuf_tensor("hB", [128, KD * NT], F32) as h_t:
                emit_attn_A(xdram, sl_xdram, z2_t, sl_z2, z2_ap,
                            wgwc_t, Region("wgwc", m_ffn1),
                            [xiA_t, xiB_t], [Region("xiA", m_ffn1), Region("xiB", m_ffn1)],
                            [z1A_t, z1B_t], [Region("z1A", m_ffn1), Region("z1B", m_ffn1)],
                            g_t, Region("g", m_ffn1),
                            b_t, Region("b", m_ffn1), h_t, Region("h", m_ffn1))
            m_attA = p.marks_now()
            with nc.sbuf_tensor("uhX", [128, KF * HT], BF16) as uh_t, \
                 nc.sbuf_tensor("poolB0", [128, KF * 128], BF16) as pB0, \
                 nc.sbuf_tensor("poolB1", [128, KF * 128], BF16) as pB1, \
                 nc.sbuf_tensor("poolB2", [128, KF * 128], BF16) as pB2:
                emit_attn_B(sl_z2, z2_ap, uh_t, Region("uh", m_attA),
                            [pB0, pB1, pB2], [Region(f"poolB{i}", m_attA) for i in range(3)])
            m_attn = p.marks_now()

        with nc.sbuf_tensor("A0b", [128, KF * HT], BF16) as A0b_t, \
             nc.sbuf_tensor("A1b", [128, KF * HT], BF16) as A1b_t, \
             nc.sbuf_tensor("zfb", [128, KD * HT], BF16) as zfb_t, \
             nc.sbuf_tensor("poolC0", [128, KF * 128], BF16) as pC0, \
             nc.sbuf_tensor("poolC1", [128, KF * 128], BF16) as pC1, \
             nc.sbuf_tensor("poolC2", [128, KF * 128], BF16) as pC2:
            reg_A2 = [Region("A0b", m_attn), Region("A1b", m_attn)]
            reg_zf2 = Region("zfb", m_attn)
            emit_ffn(xdram, sl_xdram, y_d, sl_y, [A0b_t, A1b_t], reg_A2, zfb_t, reg_zf2,
                     [pC0, pC1, pC2], [Region(f"poolC{i}", m_attn) for i in range(3)])

        for i in range(NQSEM):
            if p.cnt[f"sy{i}"]:
                p.eng("sync").wait_ge(p.sem[f"sy{i}"], p.cnt[f"sy{i}"])
            if p.cnt[f"wg{i}"]:
                p.eng("gpsimd").wait_ge(p.sem[f"wg{i}"], p.cnt[f"wg{i}"])

    return nc



def _swizzle_panels(w_io):
    I, O = w_io.shape
    return np.ascontiguousarray(
        w_io.reshape(I // 128, 128, O // 128, 128).transpose(2, 1, 0, 3)
        .reshape(O // 128, 128, I))


def make_core_inputs(xb_core, W):
    d = {"xin": np.ascontiguousarray(xb_core.T)}
    d["p_w1"] = W["p_w1"]; d["p_w2"] = W["p_w2"]; d["p_h1"] = W["p_h1"]
    d["p_h2"] = W["p_h2"]; d["p_w3"] = W["p_w3"]; d["p_wg"] = W["p_wg"]
    d["p_wc"] = W["p_wc"]; d["p_aw1"] = W["p_aw1"]; d["p_aw3"] = W["p_aw3"]
    d["p_aw2"] = W["p_aw2"]
    for k in ("v_bg", "v_bc", "v_n1", "v_n2", "v_s1", "v_s2", "v_t1", "v_t2", "v_s3"):
        d[k] = W[k]
    return d


def prep_weights(L0_attn_Wg, L0_attn_bg, L0_attn_Wc, L0_attn_bc, L0_attn_n1_w,
                 L0_attn_n2_w, L0_attn_W1, L0_attn_W3, L0_attn_W2,
                 L0_ffn_w1_w, L0_ffn_w1_s, L0_ffn_w2_w, L0_ffn_w2_s,
                 L0_ffn_h1_w, L0_ffn_h1_s, L0_ffn_h2_w, L0_ffn_h2_s,
                 L0_ffn_w3_w, L0_ffn_w3_s):
    f32 = lambda a: np.ascontiguousarray(np.asarray(a), dtype=np.float32)
    W = {}
    W["p_w1"] = _swizzle_panels(f32(L0_ffn_w1_w).T)
    W["p_w2"] = _swizzle_panels(f32(L0_ffn_w2_w).T)
    W["p_h1"] = _swizzle_panels(f32(L0_ffn_h1_w).T)
    W["p_h2"] = _swizzle_panels(f32(L0_ffn_h2_w).T)
    W["p_w3"] = _swizzle_panels(f32(L0_ffn_w3_w).T)
    W["p_wg"] = _swizzle_panels(f32(L0_attn_Wg))
    W["p_wc"] = _swizzle_panels(f32(L0_attn_Wc))
    W["p_aw1"] = _swizzle_panels(f32(L0_attn_W1))
    W["p_aw3"] = _swizzle_panels(f32(L0_attn_W3))
    W["p_aw2"] = _swizzle_panels(f32(L0_attn_W2))
    W["v_bg"] = f32(L0_attn_bg).reshape(KD, 128)
    W["v_bc"] = f32(L0_attn_bc).reshape(KD, 128)
    W["v_n1"] = f32(L0_attn_n1_w).reshape(KD, 128)
    W["v_n2"] = f32(L0_attn_n2_w).reshape(KD, 128)
    W["v_s1"] = f32(L0_ffn_w1_s).reshape(KF, 128)
    W["v_s2"] = f32(L0_ffn_w2_s).reshape(KF, 128)
    W["v_t1"] = f32(L0_ffn_h1_s).reshape(KF, 128)
    W["v_t2"] = f32(L0_ffn_h2_s).reshape(KF, 128)
    W["v_s3"] = f32(L0_ffn_w3_s).reshape(KD, 128)
    return W


def kernel(L0_attn_x, L0_attn_Wg, L0_attn_bg, L0_attn_Wc, L0_attn_bc,
           L0_attn_n1_w, L0_attn_n2_w, L0_attn_W1, L0_attn_W3, L0_attn_W2,
           L0_ffn_w1_w, L0_ffn_w1_s, L0_ffn_w2_w, L0_ffn_w2_s,
           L0_ffn_h1_w, L0_ffn_h1_s, L0_ffn_h2_w, L0_ffn_h2_s,
           L0_ffn_w3_w, L0_ffn_w3_s):
    x = np.ascontiguousarray(np.asarray(L0_attn_x), dtype=np.float32)
    B, T, _ = x.shape
    W = prep_weights(L0_attn_Wg, L0_attn_bg, L0_attn_Wc, L0_attn_bc, L0_attn_n1_w,
                     L0_attn_n2_w, L0_attn_W1, L0_attn_W3, L0_attn_W2,
                     L0_ffn_w1_w, L0_ffn_w1_s, L0_ffn_w2_w, L0_ffn_w2_s,
                     L0_ffn_h1_w, L0_ffn_h1_s, L0_ffn_h2_w, L0_ffn_h2_s,
                     L0_ffn_w3_w, L0_ffn_w3_s)
    nc = build_program(T=T)
    in_maps = [make_core_inputs(x[i], W) for i in range(B)]
    res = run_bass_kernel_spmd(nc, in_maps, core_ids=list(range(B)))
    out = np.stack([res.results[i]["y"] for i in range(B)], axis=0)
    return np.ascontiguousarray(out.transpose(0, 2, 1))


# version 07
# speedup vs baseline: 1.0234x; 1.0234x over previous
import numpy as np
import ml_dtypes
from contextlib import ExitStack

import concourse.bass as bass
import concourse.mybir as mybir
from concourse.bass_utils import run_bass_kernel_spmd

F32 = mybir.dt.float32
BF16 = mybir.dt.bfloat16
AF = mybir.ActivationFunctionType
ALU = mybir.AluOpType

D = 1024
DFF = 4096
KD = D // 128
KF = DFF // 128
NT = 512

ENG_SEM = {"tensor": "pe", "scalar": "sc", "vector": "ve", "gpsimd": "gp"}
NQSEM = 8


class Slot:
    __slots__ = ("ready", "weng", "wseq", "users", "frozen", "name")

    def __init__(self, name="", users=None, frozen=False):
        self.ready = None
        self.weng = None
        self.wseq = 0
        self.users = list(users) if users else []
        self.frozen = frozen
        self.name = name


def marks_max(slots):
    mx = {}
    for s in slots:
        for m in ([s.ready] if s.ready else []) + s.users:
            if m[1] > mx.get(m[0], 0):
                mx[m[0]] = m[1]
    return [(k, v) for k, v in mx.items()]


class Region:

    def __init__(self, name, marks0=()):
        self.name = name
        self.marks = list(marks0)
        self.cur = []

    def slot(self, name):
        s = Slot(name, users=list(self.marks))
        self.cur.append(s)
        return s

    def turn(self):
        if self.cur:
            self.marks = marks_max(self.cur)
            self.cur = []


class P:
    def __init__(self, nc, ctx):
        self.nc = nc
        names = ["pe", "sc", "ve", "gp"] + [f"sy{i}" for i in range(NQSEM)] + \
                [f"wg{i}" for i in range(NQSEM)]
        self.sem = {s: ctx.enter_context(nc.semaphore("s_" + s)) for s in names}
        self.cnt = {s: 0 for s in names}
        ENGS = ("tensor", "scalar", "vector", "sync", "gpsimd")
        self.waited = {e: {s: 0 for s in names} for e in ENGS}
        self.nseq = {e: 0 for e in ENGS}
        self.sealed = {e: 0 for e in ENGS}
        self.qi = {"sync": 0, "gpsimd": 0}
        self.bank_i = 0
        self.banks = None

    def eng(self, name):
        return getattr(self.nc, name)

    def wait(self, ename, dep):
        if dep is None:
            return
        sem, val = dep
        if val <= 0 or self.waited[ename][sem] >= val or ENG_SEM.get(ename) == sem:
            return
        self.eng(ename).wait_ge(self.sem[sem], val)
        self.waited[ename][sem] = val

    def _pre(self, ename, reads, writes):
        own = ENG_SEM.get(ename)
        need_drain = False
        for s in reads:
            if s.ready is None:
                assert s.frozen, f"read of unwritten slot {s.name}"
                continue
            if own is not None and s.ready[0] == own:
                if s.ready[1] > self.sealed[ename]:
                    need_drain = True
            else:
                self.wait(ename, s.ready)
        for s in writes:
            assert not s.frozen, f"write to frozen slot {s.name}"
            for u in s.users:
                if own is not None and u[0] == own:
                    if u[1] > self.sealed[ename]:
                        need_drain = True
                else:
                    self.wait(ename, u)
            if s.ready is not None:
                if not (s.weng == ename and own is not None and s.ready[0] == own):
                    self.wait(ename, s.ready)
        if need_drain:
            self.eng(ename).drain()
            self.sealed[ename] = self.cnt[own]

    def _post(self, ename, ins, sem, n, reads, writes):
        self.nseq[ename] += 1
        ins.then_inc(self.sem[sem], n)
        self.cnt[sem] += n
        mark = (sem, self.cnt[sem])
        for s in reads:
            if not s.frozen:
                s.users.append(mark)
        for s in writes:
            s.ready = mark
            s.weng = ename
            s.wseq = self.nseq[ename]
            s.users = []
        return mark

    def op(self, ename, emit, reads=(), writes=()):
        self._pre(ename, reads, writes)
        ins = emit(self.eng(ename))
        return self._post(ename, ins, ENG_SEM[ename], 1, reads, writes)

    def dma(self, q, out_ap, in_ap, reads=(), writes=(), **kw):
        pfx = "sy" if q == "sync" else "wg"
        sem = f"{pfx}{self.qi[q] % NQSEM}"
        self.qi[q] += 1
        self.wait(q, (sem, self.cnt[sem]))
        self._pre(q, reads, writes)
        ins = self.eng(q).dma_start(out_ap, in_ap, **kw)
        return self._post(q, ins, sem, 16, reads, writes)

    def mm_group(self, bank, mms, start=True, stop=True):
        ename = "tensor"
        if start:
            for u in bank.users:
                self.wait(ename, u)
        n = len(mms)
        for i, (l, r, rds) in enumerate(mms):
            for s in rds:
                if s.ready is not None:
                    self.wait(ename, s.ready)
            ins = self.nc.tensor.matmul(bank.ap, l, r, start=(start and i == 0),
                                        stop=(stop and i == n - 1))
            self.nseq[ename] += 1
            if i == n - 1:
                ins.then_inc(self.sem["pe"], 1)
                self.cnt["pe"] += 1
        mark = ("pe", self.cnt["pe"])
        if stop:
            bank.ready = mark
            bank.weng = ename
            bank.users = []
        for _, _, rds in mms:
            for s in rds:
                if not s.frozen:
                    s.users.append(mark)
        return mark

    def next_bank(self):
        b = self.banks[self.bank_i % 8]
        self.bank_i += 1
        return b

    def marks_now(self):
        return [(s, v) for s, v in self.cnt.items() if v > 0]


def build_program(T=2048):
    HT = T // 2
    NH = HT // NT
    NCH = T // NT
    nc = bass.Bass("TRN2", target_bir_lowering=False)

    xin = nc.dram_tensor("xin", [D, T], F32, kind="ExternalInput").ap()
    y_d = nc.dram_tensor("y", [D, T], F32, kind="ExternalOutput").ap()
    xdram = nc.dram_tensor("xdram", [D, T], F32).ap()

    wp = {}
    for name, mch, kch in [("p_w1", KF, KD), ("p_w2", KF, KD), ("p_h1", KF, KF),
                           ("p_h2", KF, KF), ("p_w3", KD, KF), ("p_wg", KD, KD),
                           ("p_wc", KD, KD), ("p_aw1", KF, KD), ("p_aw3", KF, KD),
                           ("p_aw2", KD, KF)]:
        wp[name] = nc.dram_tensor(name, [mch, 128, kch * 128], F32,
                                  kind="ExternalInput").ap()
    vin = {}
    for name, chk in [("v_bg", KD), ("v_bc", KD), ("v_n1", KD), ("v_n2", KD),
                      ("v_s1", KF), ("v_s2", KF), ("v_t1", KF), ("v_t2", KF),
                      ("v_s3", KD)]:
        vin[name] = nc.dram_tensor(name, [chk, 128], F32, kind="ExternalInput").ap()

    def x_kview(xap, t0, tl):
        return xap.rearrange("(k p) t -> p k t", p=128)[:, :, t0:t0 + tl]

    with ExitStack() as ctx:
        p = P(nc, ctx)
        sb = lambda name, shape, dt: ctx.enter_context(nc.sbuf_tensor(name, shape, dt))

        ones_bf = sb("ones_bf", [128, 128], BF16)
        consts = sb("consts", [128, 4], F32)
        t_bg = sb("t_bg", [128, KD], F32)
        t_bcn = sb("t_bcn", [128, KD], F32)
        t_n1 = sb("t_n1", [128, KD], F32)
        t_n2 = sb("t_n2", [128, KD], F32)
        t_s1 = sb("t_s1", [128, KF], F32)
        t_s2 = sb("t_s2", [128, KF], F32)
        t_t1 = sb("t_t1", [128, KF], F32)
        t_t2 = sb("t_t2", [128, KF], F32)
        t_s3h = sb("t_s3h", [128, KD], F32)
        xb_t = [sb(f"xb{i}", [128, KD * NT], BF16) for i in range(2)]
        xsq_t = [sb(f"xsq{i}", [128, NT], BF16) for i in range(2)]
        tf_t = [sb(f"tf{i}", [128, NT], F32) for i in range(4)]
        rs_t = [sb(f"rs{i}", [128, NT], F32) for i in range(2)]
        mr_t = [sb(f"mr{i}", [128, NT], F32) for i in range(2)]
        xt_t = [sb(f"xt{i}", [128, NT], F32) for i in range(2)]
        pb_t = [ctx.enter_context(nc.psum_tensor(f"pb{i}", [128, NT], F32))
                for i in range(8)]

        EPS_LN = consts[:, 0:1]
        EPS_RMS = consts[:, 1:2]
        NEG_HALF = consts[:, 2:3]
        NEG_ONE = consts[:, 3:4]

        class BankSlot(Slot):
            __slots__ = ("ap",)
        banks = []
        for i in range(8):
            s = BankSlot(f"pb{i}")
            s.ap = pb_t[i][:, :]
            banks.append(s)
        p.banks = banks

        sl_ones = Slot("ones")
        sl_consts = Slot("consts")
        sl_vec = {n: Slot(n) for n in
                  ("bg", "bcn", "n1", "n2", "s1", "s2", "t1", "t2", "s3h")}
        sl_xb = [Slot(f"xb{i}") for i in range(2)]
        sl_xsq = [Slot(f"xsq{i}") for i in range(2)]
        sl_tf = [Slot(f"tf{i}") for i in range(4)]
        sl_rs = [Slot(f"rs{i}") for i in range(2)]
        sl_mr = [Slot(f"mr{i}") for i in range(2)]
        sl_xt = [Slot(f"xt{i}") for i in range(2)]
        sl_xdram = [[Slot(f"xd{r}_{t}") for t in range(NCH)] for r in range(KD)]
        sl_y = [[Slot(f"yy{r}_{t}") for t in range(NCH)] for r in range(KD)]

        block = ctx.enter_context(nc.Block())

        p.op("vector", lambda e: e.memset(ones_bf[:, :], 1.0 / D), writes=[sl_ones])
        p.op("vector", lambda e: e.memset(consts[:, 0:1], 1e-5), writes=[sl_consts])
        p.op("vector", lambda e: e.memset(consts[:, 1:2], 1e-6), writes=[sl_consts])
        p.op("vector", lambda e: e.memset(consts[:, 2:3], -0.5), writes=[sl_consts])
        p.op("vector", lambda e: e.memset(consts[:, 3:4], -1.0), writes=[sl_consts])
        sl_consts.frozen = True
        sl_ones.frozen = True
        with nc.allow_non_contiguous_dma(reason="tiny vector loads"):
            for nm, tile, slot in [("v_bg", t_bg, sl_vec["bg"]), ("v_bc", t_bcn, sl_vec["bcn"]),
                                   ("v_n1", t_n1, sl_vec["n1"]), ("v_n2", t_n2, sl_vec["n2"]),
                                   ("v_s1", t_s1, sl_vec["s1"]), ("v_s2", t_s2, sl_vec["s2"]),
                                   ("v_t1", t_t1, sl_vec["t1"]), ("v_t2", t_t2, sl_vec["t2"]),
                                   ("v_s3", t_s3h, sl_vec["s3h"])]:
                p.dma("sync", tile[:, :], vin[nm].transpose([1, 0]), writes=[slot])
        p.op("vector", lambda e: e.tensor_scalar_mul(t_bcn[:, :], t_bcn[:, :], -1.0),
             reads=[sl_vec["bcn"]], writes=[sl_vec["bcn"]])
        p.op("vector", lambda e: e.tensor_scalar_mul(t_s3h[:, :], t_s3h[:, :], 0.5),
             reads=[sl_vec["s3h"]], writes=[sl_vec["s3h"]])
        for s in sl_vec.values():
            s.frozen = True

        def load_xb(src_ap, src_slots, t0, bi):
            dst = xb_t[bi][:, :].rearrange("p (k t) -> p k t", t=NT)
            p.dma("gpsimd", dst, x_kview(src_ap, t0, NT),
                  reads=src_slots, writes=[sl_xb[bi]])
            return sl_xb[bi], xb_t[bi]

        def stats(xbslot, xb_tile, kind, si, sq_src=None, f1_out=None):
            xb_ap = lambda k: xb_tile[:, k * NT:(k + 1) * NT]
            sqs, sq_ap = (xbslot, xb_ap) if sq_src is None else sq_src
            bank_m = None
            if kind == "ln":
                bank_m = p.next_bank()
                p.mm_group(bank_m, [(ones_bf[:, :], xb_ap(k), (xbslot, sl_ones))
                                    for k in range(KD)])
            bank_q = p.next_bank()
            for batch in range(KD // 2):
                mms = []
                for j in range(2):
                    k = batch * 2 + j
                    p.op("scalar", lambda e, k=k, j=j:
                         e.activation(xsq_t[j][:, :], sq_ap(k), AF.Square),
                         reads=[sqs], writes=[sl_xsq[j]])
                    mms.append((ones_bf[:, :], xsq_t[j][:, :], (sl_xsq[j], sl_ones)))
                p.mm_group(bank_q, mms, start=(batch == 0), stop=(batch == KD // 2 - 1))
            r_sl, r_ap = sl_rs[si], rs_t[si][:, :]
            if kind == "ln":
                mr_sl, mr_ap = sl_mr[si], mr_t[si][:, :]
                p.op("scalar", lambda e: e.activation(mr_ap, bank_m.ap, AF.Square),
                     reads=[bank_m], writes=[mr_sl])
                p.op("vector", lambda e: e.tensor_sub(mr_ap, bank_q.ap, mr_ap),
                     reads=[bank_q, mr_sl], writes=[mr_sl])
                p.op("scalar", lambda e: e.activation(r_ap, mr_ap, AF.Ln, bias=EPS_LN),
                     reads=[mr_sl, sl_consts], writes=[r_sl])
                p.op("scalar", lambda e: e.activation(r_ap, r_ap, AF.Exp, scale=NEG_HALF),
                     reads=[r_sl, sl_consts], writes=[r_sl])
                if f1_out is not None:
                    f1sl, f1ap = f1_out
                    p.op("vector", lambda e: e.tensor_mul(f1ap, mr_ap, r_ap),
                         reads=[mr_sl, r_sl], writes=[f1sl])
                    p.op("vector", lambda e: e.tensor_mul(f1ap, f1ap, r_ap),
                         reads=[f1sl, r_sl], writes=[f1sl])
                    p.op("scalar", lambda e: e.activation(f1ap, f1ap, AF.Ln, bias=EPS_RMS),
                         reads=[f1sl, sl_consts], writes=[f1sl])
                    p.op("scalar", lambda e: e.activation(f1ap, f1ap, AF.Exp, scale=NEG_HALF),
                         reads=[f1sl, sl_consts], writes=[f1sl])
                return r_sl, r_ap, mr_sl, bank_m
            else:
                p.op("scalar", lambda e: e.activation(r_ap, bank_q.ap, AF.Ln, bias=EPS_RMS),
                     reads=[bank_q, sl_consts], writes=[r_sl])
                p.op("scalar", lambda e: e.activation(r_ap, r_ap, AF.Exp, scale=NEG_HALF),
                     reads=[r_sl, sl_consts], writes=[r_sl])
                return r_sl, r_ap, None, None

        def ln_apply(in_f, r_sl, r_ap, bank_m, out_f):
            for h4 in range(2):
                ks = range(h4 * 4, h4 * 4 + 4)
                for k in ks:
                    iap, irds = in_f(k)
                    p.op("vector", lambda e, k=k, iap=iap:
                         e.tensor_sub(tf_t[k % 4][:, :], iap, bank_m.ap),
                         reads=list(irds) + [bank_m], writes=[sl_tf[k % 4]])
                for k in ks:
                    oap, osl = out_f(k)
                    p.op("vector", lambda e, k=k, oap=oap:
                         e.tensor_mul(oap, tf_t[k % 4][:, :], r_ap),
                         reads=[sl_tf[k % 4], r_sl], writes=[osl])

        def emit_ffn(src_ap, src_slots2d, dst_ap, dst_slots2d, A_t, reg_A, zf_t, reg_zf,
                     pool_t, reg_pool):
            zf_ap = lambda k, t: zf_t[:, k * HT + t * NT: k * HT + t * NT + NT]
            zf_state = {}

            def do_s1(half):
                th0 = half * HT
                reg_zf.turn()
                sl_zf = [[reg_zf.slot(f"zf{half}_{k}_{t}") for t in range(NH)]
                         for k in range(KD)]
                for tt in range(NH):
                    t0 = th0 + tt * NT
                    ch = t0 // NT
                    xbs, xbt = load_xb(src_ap, [src_slots2d[r][ch] for r in range(KD)],
                                       t0, tt % 2)
                    r_sl, r_ap, mr_sl, bank_m = stats(xbs, xbt, "ln", tt % 2)
                    ln_apply(lambda k: (xbt[:, k * NT:(k + 1) * NT], (xbs,)),
                             r_sl, r_ap, bank_m,
                             lambda k: (zf_ap(k, tt), sl_zf[k][tt]))
                zf_state[half] = sl_zf

            do_s1(0)
            for half in range(2):
                th0 = half * HT
                u_i, h1_i = half % 2, (half + 1) % 2
                sl_zf = zf_state[half]

                reg_A[u_i].turn()
                sl_u = [[reg_A[u_i].slot(f"u{half}_{m}_{t}") for t in range(NH)]
                        for m in range(KF)]
                u_ap = lambda m, t: A_t[u_i][:, m * HT + t * NT: m * HT + t * NT + NT]
                for rp in reg_pool[:2]:
                    rp.turn()
                for m in range(KF):
                    pi = m % 2
                    pt = pool_t[pi]
                    if m >= 2:
                        reg_pool[pi].turn()
                    w1sl = reg_pool[pi].slot(f"w1_{half}_{m}")
                    w2sl = reg_pool[pi].slot(f"w2_{half}_{m}")
                    p.dma("gpsimd", pt[:, 0:KD * 128], wp["p_w1"][m], writes=[w1sl])
                    p.dma("gpsimd", pt[:, KD * 128:2 * KD * 128], wp["p_w2"][m], writes=[w2sl])
                    for t in range(NH):
                        b1 = p.next_bank()
                        b2 = p.next_bank()
                        p.mm_group(b1, [(pt[:, k * 128:(k + 1) * 128], zf_ap(k, t),
                                         (w1sl, sl_zf[k][t])) for k in range(KD)])
                        p.mm_group(b2, [(pt[:, KD * 128 + k * 128:KD * 128 + (k + 1) * 128],
                                         zf_ap(k, t), (w2sl, sl_zf[k][t])) for k in range(KD)])
                        ti = (m * NH + t) % 2
                        p.op("scalar", lambda e, b1=b1, m=m, ti=ti:
                             e.activation(tf_t[ti][:, :], b1.ap, AF.Silu, scale=t_s1[:, m:m + 1]),
                             reads=[b1, sl_vec["s1"]], writes=[sl_tf[ti]])
                        p.op("vector", lambda e, b2=b2, m=m, t=t, ti=ti:
                             e.scalar_tensor_tensor(u_ap(m, t), b2.ap, t_s2[:, m:m + 1],
                                                    tf_t[ti][:, :], ALU.mult, ALU.mult),
                             reads=[b2, sl_tf[ti], sl_vec["s2"]], writes=[sl_u[m][t]])

                def hstage(wname, tvec_t, tvec_sl, in_ap, in_sl, out_i, tag):
                    reg_A[out_i].turn()
                    o_sl = [[reg_A[out_i].slot(f"{tag}_{m}_{t}") for t in range(NH)]
                            for m in range(KF)]
                    o_ap = lambda m, t: A_t[out_i][:, m * HT + t * NT: m * HT + t * NT + NT]
                    for m in range(KF):
                        pi = m % 3
                        reg_pool[pi].turn()
                        wsl = reg_pool[pi].slot(f"{tag}w_{m}")
                        p.dma("gpsimd", pool_t[pi][:, :], wp[wname][m], writes=[wsl])
                        for t in range(NH):
                            b = p.next_bank()
                            p.mm_group(b, [(pool_t[pi][:, k * 128:(k + 1) * 128],
                                            in_ap(k, t), (wsl, in_sl[k][t]))
                                           for k in range(KF)])
                            p.op("scalar", lambda e, b=b, m=m, t=t:
                                 e.activation(o_ap(m, t), b.ap, AF.Gelu,
                                              scale=tvec_t[:, m:m + 1]),
                                 reads=[b, tvec_sl], writes=[o_sl[m][t]])
                    return o_sl, o_ap

                sl_h1, h1_ap = hstage("p_h1", t_t1, sl_vec["t1"], u_ap, sl_u, h1_i, f"h1_{half}")
                if half == 0:
                    do_s1(1)
                sl_h2, h2_ap = hstage("p_h2", t_t2, sl_vec["t2"], h1_ap, sl_h1, u_i, f"h2_{half}")

                for mo in range(KD):
                    pi = mo % 3
                    reg_pool[pi].turn()
                    wsl = reg_pool[pi].slot(f"w3_{mo}")
                    p.dma("gpsimd", pool_t[pi][:, :], wp["p_w3"][mo], writes=[wsl])
                    for t in range(NH):
                        ch = (th0 + t * NT) // NT
                        b = p.next_bank()
                        p.mm_group(b, [(pool_t[pi][:, k * 128:(k + 1) * 128],
                                        h2_ap(k, t), (wsl, sl_h2[k][t]))
                                       for k in range(KF)])
                        xi_ = (mo * NH + t) % 2
                        xs = sl_xt[xi_]
                        p.dma("sync", xt_t[xi_][:, :],
                              src_ap[mo * 128:(mo + 1) * 128, th0 + t * NT: th0 + (t + 1) * NT],
                              reads=[src_slots2d[mo][ch]], writes=[xs])
                        p.op("vector", lambda e, b=b, mo=mo, xi_=xi_:
                             e.scalar_tensor_tensor(xt_t[xi_][:, :], b.ap, t_s3h[:, mo:mo + 1],
                                                    xt_t[xi_][:, :], ALU.mult, ALU.add),
                             reads=[b, xs, sl_vec["s3h"]], writes=[xs])
                        p.dma("sync",
                              dst_ap[mo * 128:(mo + 1) * 128, th0 + t * NT: th0 + (t + 1) * NT],
                              xt_t[xi_][:, :], reads=[xs], writes=[dst_slots2d[mo][ch]])

        def emit_attn_A(src_ap, src_slots2d, z2_t, sl_z2, z2_ap, wgwc_t, reg_wgwc,
                        xi_t, reg_xi, z1_t, reg_z1, g_t, reg_g, b_t, reg_b,
                        h_t, reg_h):
            sl_wg = [reg_wgwc.slot(f"wg{m}") for m in range(KD)]
            sl_wc = [reg_wgwc.slot(f"wc{m}") for m in range(KD)]
            wg_ap = lambda m, k: wgwc_t[:, m * 1024 + k * 128: m * 1024 + (k + 1) * 128]
            wc_ap = lambda m, k: wgwc_t[:, 8192 + m * 1024 + k * 128: 8192 + m * 1024 + (k + 1) * 128]
            dstg = wgwc_t[:, 0:8192].rearrange("p (m c) -> p m c", c=1024)
            dstc = wgwc_t[:, 8192:16384].rearrange("p (m c) -> p m c", c=1024)
            p.dma("gpsimd", dstg, wp["p_wg"].transpose([1, 0, 2]), writes=sl_wg)
            p.dma("gpsimd", dstc, wp["p_wc"].transpose([1, 0, 2]), writes=sl_wc)
            sl_g = sl_b = sl_h = None

            def issue_loads(c):
                bi = c % 2
                reg_xi[bi].turn()
                xisl = reg_xi[bi].slot(f"xi{c}")
                src_sl = [src_slots2d[r][c] for r in range(KD)]
                p.dma("sync", xi_t[bi][:, :].rearrange("p (k t) -> p k t", t=NT),
                      x_kview(src_ap, c * NT, NT), reads=src_sl, writes=[xisl])
                xbs, xbt = load_xb(src_ap, src_sl, c * NT, bi)
                return xisl, xbs, xbt

            pend = issue_loads(0)
            for c in range(NCH):
                t0 = c * NT
                bi = c % 2
                xisl, xbs, xbt = pend
                xi_ap = lambda k, bi=bi: xi_t[bi][:, k * NT:(k + 1) * NT]
                if c + 1 < NCH:
                    pend = issue_loads(c + 1)
                r_sl, r_ap, mr_sl, bank_m = stats(
                    xbs, xbt, "ln", bi, f1_out=(sl_mr[bi], mr_t[bi][:, :]))
                for h4 in range(2):
                    ks = range(h4 * 4, h4 * 4 + 4)
                    for k in ks:
                        p.op("vector", lambda e, k=k: e.tensor_sub(
                            tf_t[k % 4][:, :], xi_ap(k), bank_m.ap),
                            reads=[xisl, bank_m], writes=[sl_tf[k % 4]])
                    for k in ks:
                        p.op("gpsimd", lambda e, k=k: e.tensor_mul(
                            xi_ap(k), tf_t[k % 4][:, :], r_ap),
                            reads=[sl_tf[k % 4], r_sl], writes=[xisl])
                reg_z1[bi].turn()
                z1sl = reg_z1[bi].slot(f"z1_{c}")
                z1_ap = lambda k, bi=bi: z1_t[bi][:, k * NT:(k + 1) * NT]
                for k in range(KD):
                    p.op("vector", lambda e, k=k, bi=bi:
                         e.scalar_tensor_tensor(z1_ap(k), xi_ap(k), t_n1[:, k:k + 1],
                                                mr_t[bi][:, :], ALU.mult, ALU.mult),
                         reads=[xisl, sl_mr[bi], sl_vec["n1"]], writes=[z1sl])
                prev_h = sl_h
                reg_g.turn(); reg_b.turn()
                sl_g = [reg_g.slot(f"g{c}_{m}") for m in range(KD)]
                sl_b = [reg_b.slot(f"b{c}_{m}") for m in range(KD)]
                for m in range(KD):
                    bg_ = p.next_bank()
                    bc_ = p.next_bank()
                    p.mm_group(bg_, [(wg_ap(m, k), z1_ap(k), (sl_wg[m], z1sl))
                                     for k in range(KD)])
                    p.mm_group(bc_, [(wc_ap(m, k), z1_ap(k), (sl_wc[m], z1sl))
                                     for k in range(KD)])
                    p.op("scalar", lambda e, bg_=bg_, m=m:
                         e.activation(g_t[:, m * NT:(m + 1) * NT], bg_.ap, AF.Sigmoid,
                                      bias=t_bg[:, m:m + 1]),
                         reads=[bg_, sl_vec["bg"]], writes=[sl_g[m]])
                    p.op("scalar", lambda e, bc_=bc_, m=m:
                         e.activation(b_t[:, m * NT:(m + 1) * NT], bc_.ap, AF.Tanh,
                                      scale=NEG_ONE, bias=t_bcn[:, m:m + 1]),
                         reads=[bc_, sl_vec["bcn"], sl_consts], writes=[sl_b[m]])
                    p.op("vector", lambda e, m=m:
                         e.scalar_tensor_tensor(b_t[:, m * NT:(m + 1) * NT],
                                                g_t[:, m * NT:(m + 1) * NT], 1.0,
                                                b_t[:, m * NT:(m + 1) * NT],
                                                ALU.subtract, ALU.mult),
                         reads=[sl_g[m], sl_b[m]], writes=[sl_b[m]])
                reg_h.turn()
                new_h = [reg_h.slot(f"h{c}_{f}") for f in range(KD)]
                for f in range(KD):
                    if c == 0:
                        init = 0.0
                        rds = [sl_g[f], sl_b[f]]
                    else:
                        init = h_t[:, f * NT + NT - 1: f * NT + NT]
                        rds = [sl_g[f], sl_b[f], prev_h[f]]
                    p.op("vector", lambda e, f=f, init=init:
                         e.tensor_tensor_scan(h_t[:, f * NT:(f + 1) * NT],
                                              g_t[:, f * NT:(f + 1) * NT],
                                              b_t[:, f * NT:(f + 1) * NT],
                                              init, ALU.mult, ALU.add),
                         reads=rds, writes=[new_h[f]])
                sl_h = new_h
                p.op("gpsimd", lambda e, bi=bi: e.tensor_add(xi_t[bi][:, :], xi_t[bi][:, :],
                                                             h_t[:, :]),
                     reads=[xisl] + sl_h, writes=[xisl])
                r2_sl, r2_ap, _, _ = stats(None, xb_t[bi], "rms", bi,
                                           sq_src=(xisl, xi_ap))
                for k in range(KD):
                    p.op("vector", lambda e, k=k, c=c:
                         e.scalar_tensor_tensor(z2_ap(k, c), xi_ap(k), t_n2[:, k:k + 1],
                                                r2_ap, ALU.mult, ALU.mult),
                         reads=[xisl, r2_sl, sl_vec["n2"]], writes=[sl_z2[k][c]])
                p.dma("gpsimd", x_kview(xdram, t0, NT),
                      xi_t[bi][:, :].rearrange("p (k t) -> p k t", t=NT),
                      reads=[xisl] + [sl_xdram[r][c] for r in range(KD)],
                      writes=[sl_xdram[r][c] for r in range(KD)],
                      accum_op=ALU.add)

        def emit_attn_B(sl_z2, z2_ap, uh_t, reg_uh, pool_t, reg_pool):
            for hb in range(2):
                th0 = hb * HT
                reg_uh.turn()
                sl_uh = [[reg_uh.slot(f"uh{hb}_{m}_{t}") for t in range(NH)]
                         for m in range(KF)]
                uh_ap = lambda m, t: uh_t[:, m * HT + t * NT: m * HT + t * NT + NT]
                for rp in reg_pool[:2]:
                    rp.turn()
                for m in range(KF):
                    pi = m % 2
                    pt = pool_t[pi]
                    if m >= 2:
                        reg_pool[pi].turn()
                    w1sl = reg_pool[pi].slot(f"aw1_{hb}_{m}")
                    w3sl = reg_pool[pi].slot(f"aw3_{hb}_{m}")
                    p.dma("gpsimd", pt[:, 0:KD * 128], wp["p_aw1"][m], writes=[w1sl])
                    p.dma("gpsimd", pt[:, KD * 128:2 * KD * 128], wp["p_aw3"][m], writes=[w3sl])
                    for t in range(NH):
                        c = hb * NH + t
                        b1 = p.next_bank()
                        b3 = p.next_bank()
                        p.mm_group(b1, [(pt[:, k * 128:(k + 1) * 128], z2_ap(k, c),
                                         (w1sl, sl_z2[k][c])) for k in range(KD)])
                        p.mm_group(b3, [(pt[:, KD * 128 + k * 128:KD * 128 + (k + 1) * 128],
                                         z2_ap(k, c), (w3sl, sl_z2[k][c])) for k in range(KD)])
                        ti = (m * NH + t) % 2
                        p.op("scalar", lambda e, b1=b1, ti=ti:
                             e.activation(tf_t[ti][:, :], b1.ap, AF.Silu),
                             reads=[b1], writes=[sl_tf[ti]])
                        p.op("vector", lambda e, b3=b3, m=m, t=t, ti=ti:
                             e.tensor_mul(uh_ap(m, t), tf_t[ti][:, :], b3.ap),
                             reads=[b3, sl_tf[ti]], writes=[sl_uh[m][t]])
                for mo in range(KD):
                    pi = mo % 3
                    reg_pool[pi].turn()
                    wsl = reg_pool[pi].slot(f"aw2_{mo}")
                    p.dma("gpsimd", pool_t[pi][:, :], wp["p_aw2"][mo], writes=[wsl])
                    for t in range(NH):
                        c = hb * NH + t
                        b = p.next_bank()
                        p.mm_group(b, [(pool_t[pi][:, k * 128:(k + 1) * 128],
                                        uh_ap(k, t), (wsl, sl_uh[k][t]))
                                       for k in range(KF)])
                        xi_ = (mo * NH + t) % 2
                        xs = sl_xt[xi_]
                        p.dma("sync", xt_t[xi_][:, :],
                              xdram[mo * 128:(mo + 1) * 128, th0 + t * NT: th0 + (t + 1) * NT],
                              reads=[sl_xdram[mo][c]], writes=[xs])
                        p.op("vector", lambda e, b=b, xi_=xi_:
                             e.tensor_add(xt_t[xi_][:, :], xt_t[xi_][:, :], b.ap),
                             reads=[b, xs], writes=[xs])
                        p.dma("sync",
                              xdram[mo * 128:(mo + 1) * 128, th0 + t * NT: th0 + (t + 1) * NT],
                              xt_t[xi_][:, :], reads=[xs], writes=[sl_xdram[mo][c]])

        xin_slots = [[Slot(f"xin{r}_{t}", frozen=True) for t in range(NCH)]
                     for r in range(KD)]

        with nc.sbuf_tensor("A0", [128, KF * HT], BF16) as A0_t, \
             nc.sbuf_tensor("A1", [128, KF * HT], BF16) as A1_t, \
             nc.sbuf_tensor("zf", [128, KD * HT], BF16) as zf_t, \
             nc.sbuf_tensor("poolA0", [128, KF * 128], BF16) as pA0, \
             nc.sbuf_tensor("poolA1", [128, KF * 128], BF16) as pA1, \
             nc.sbuf_tensor("poolA2", [128, KF * 128], BF16) as pA2:
            reg_A = [Region("A0"), Region("A1")]
            reg_zf = Region("zf")
            emit_ffn(xin, xin_slots, xdram, sl_xdram, [A0_t, A1_t], reg_A, zf_t, reg_zf,
                     [pA0, pA1, pA2], [Region(f"poolA{i}") for i in range(3)])
            m_ffn1 = p.marks_now()

        with nc.sbuf_tensor("z2", [128, KD * T], BF16) as z2_t:
            reg_z2 = Region("z2", m_ffn1)
            sl_z2 = [[reg_z2.slot(f"z2_{k}_{c}") for c in range(NCH)] for k in range(KD)]
            z2_ap = lambda k, c: z2_t[:, k * T + c * NT: k * T + c * NT + NT]
            with nc.sbuf_tensor("wgwc", [128, 2 * KD * 1024], BF16) as wgwc_t, \
                 nc.sbuf_tensor("xiA", [128, KD * NT], F32) as xiA_t, \
                 nc.sbuf_tensor("xiB", [128, KD * NT], F32) as xiB_t, \
                 nc.sbuf_tensor("z1A", [128, KD * NT], BF16) as z1A_t, \
                 nc.sbuf_tensor("z1B", [128, KD * NT], BF16) as z1B_t, \
                 nc.sbuf_tensor("gB", [128, KD * NT], F32) as g_t, \
                 nc.sbuf_tensor("bB", [128, KD * NT], F32) as b_t, \
                 nc.sbuf_tensor("hB", [128, KD * NT], F32) as h_t:
                emit_attn_A(xdram, sl_xdram, z2_t, sl_z2, z2_ap,
                            wgwc_t, Region("wgwc", m_ffn1),
                            [xiA_t, xiB_t], [Region("xiA", m_ffn1), Region("xiB", m_ffn1)],
                            [z1A_t, z1B_t], [Region("z1A", m_ffn1), Region("z1B", m_ffn1)],
                            g_t, Region("g", m_ffn1),
                            b_t, Region("b", m_ffn1), h_t, Region("h", m_ffn1))
            m_attA = p.marks_now()
            with nc.sbuf_tensor("uhX", [128, KF * HT], BF16) as uh_t, \
                 nc.sbuf_tensor("poolB0", [128, KF * 128], BF16) as pB0, \
                 nc.sbuf_tensor("poolB1", [128, KF * 128], BF16) as pB1, \
                 nc.sbuf_tensor("poolB2", [128, KF * 128], BF16) as pB2:
                emit_attn_B(sl_z2, z2_ap, uh_t, Region("uh", m_attA),
                            [pB0, pB1, pB2], [Region(f"poolB{i}", m_attA) for i in range(3)])
            m_attn = p.marks_now()

        with nc.sbuf_tensor("A0b", [128, KF * HT], BF16) as A0b_t, \
             nc.sbuf_tensor("A1b", [128, KF * HT], BF16) as A1b_t, \
             nc.sbuf_tensor("zfb", [128, KD * HT], BF16) as zfb_t, \
             nc.sbuf_tensor("poolC0", [128, KF * 128], BF16) as pC0, \
             nc.sbuf_tensor("poolC1", [128, KF * 128], BF16) as pC1, \
             nc.sbuf_tensor("poolC2", [128, KF * 128], BF16) as pC2:
            reg_A2 = [Region("A0b", m_attn), Region("A1b", m_attn)]
            reg_zf2 = Region("zfb", m_attn)
            emit_ffn(xdram, sl_xdram, y_d, sl_y, [A0b_t, A1b_t], reg_A2, zfb_t, reg_zf2,
                     [pC0, pC1, pC2], [Region(f"poolC{i}", m_attn) for i in range(3)])

        for i in range(NQSEM):
            if p.cnt[f"sy{i}"]:
                p.eng("sync").wait_ge(p.sem[f"sy{i}"], p.cnt[f"sy{i}"])
            if p.cnt[f"wg{i}"]:
                p.eng("gpsimd").wait_ge(p.sem[f"wg{i}"], p.cnt[f"wg{i}"])

    return nc



def _swizzle_panels(w_io):
    I, O = w_io.shape
    return np.ascontiguousarray(
        w_io.reshape(I // 128, 128, O // 128, 128).transpose(2, 1, 0, 3)
        .reshape(O // 128, 128, I))


def make_core_inputs(xb_core, W):
    d = {"xin": np.ascontiguousarray(xb_core.T)}
    d["p_w1"] = W["p_w1"]; d["p_w2"] = W["p_w2"]; d["p_h1"] = W["p_h1"]
    d["p_h2"] = W["p_h2"]; d["p_w3"] = W["p_w3"]; d["p_wg"] = W["p_wg"]
    d["p_wc"] = W["p_wc"]; d["p_aw1"] = W["p_aw1"]; d["p_aw3"] = W["p_aw3"]
    d["p_aw2"] = W["p_aw2"]
    for k in ("v_bg", "v_bc", "v_n1", "v_n2", "v_s1", "v_s2", "v_t1", "v_t2", "v_s3"):
        d[k] = W[k]
    return d


def prep_weights(L0_attn_Wg, L0_attn_bg, L0_attn_Wc, L0_attn_bc, L0_attn_n1_w,
                 L0_attn_n2_w, L0_attn_W1, L0_attn_W3, L0_attn_W2,
                 L0_ffn_w1_w, L0_ffn_w1_s, L0_ffn_w2_w, L0_ffn_w2_s,
                 L0_ffn_h1_w, L0_ffn_h1_s, L0_ffn_h2_w, L0_ffn_h2_s,
                 L0_ffn_w3_w, L0_ffn_w3_s):
    f32 = lambda a: np.ascontiguousarray(np.asarray(a), dtype=np.float32)
    W = {}
    W["p_w1"] = _swizzle_panels(f32(L0_ffn_w1_w).T)
    W["p_w2"] = _swizzle_panels(f32(L0_ffn_w2_w).T)
    W["p_h1"] = _swizzle_panels(f32(L0_ffn_h1_w).T)
    W["p_h2"] = _swizzle_panels(f32(L0_ffn_h2_w).T)
    W["p_w3"] = _swizzle_panels(f32(L0_ffn_w3_w).T)
    W["p_wg"] = _swizzle_panels(f32(L0_attn_Wg))
    W["p_wc"] = _swizzle_panels(f32(L0_attn_Wc))
    W["p_aw1"] = _swizzle_panels(f32(L0_attn_W1))
    W["p_aw3"] = _swizzle_panels(f32(L0_attn_W3))
    W["p_aw2"] = _swizzle_panels(f32(L0_attn_W2))
    W["v_bg"] = f32(L0_attn_bg).reshape(KD, 128)
    W["v_bc"] = f32(L0_attn_bc).reshape(KD, 128)
    W["v_n1"] = f32(L0_attn_n1_w).reshape(KD, 128)
    W["v_n2"] = f32(L0_attn_n2_w).reshape(KD, 128)
    W["v_s1"] = f32(L0_ffn_w1_s).reshape(KF, 128)
    W["v_s2"] = f32(L0_ffn_w2_s).reshape(KF, 128)
    W["v_t1"] = f32(L0_ffn_h1_s).reshape(KF, 128)
    W["v_t2"] = f32(L0_ffn_h2_s).reshape(KF, 128)
    W["v_s3"] = f32(L0_ffn_w3_s).reshape(KD, 128)
    return W


def kernel(L0_attn_x, L0_attn_Wg, L0_attn_bg, L0_attn_Wc, L0_attn_bc,
           L0_attn_n1_w, L0_attn_n2_w, L0_attn_W1, L0_attn_W3, L0_attn_W2,
           L0_ffn_w1_w, L0_ffn_w1_s, L0_ffn_w2_w, L0_ffn_w2_s,
           L0_ffn_h1_w, L0_ffn_h1_s, L0_ffn_h2_w, L0_ffn_h2_s,
           L0_ffn_w3_w, L0_ffn_w3_s):
    x = np.ascontiguousarray(np.asarray(L0_attn_x), dtype=np.float32)
    B, T, _ = x.shape
    W = prep_weights(L0_attn_Wg, L0_attn_bg, L0_attn_Wc, L0_attn_bc, L0_attn_n1_w,
                     L0_attn_n2_w, L0_attn_W1, L0_attn_W3, L0_attn_W2,
                     L0_ffn_w1_w, L0_ffn_w1_s, L0_ffn_w2_w, L0_ffn_w2_s,
                     L0_ffn_h1_w, L0_ffn_h1_s, L0_ffn_h2_w, L0_ffn_h2_s,
                     L0_ffn_w3_w, L0_ffn_w3_s)
    nc = build_program(T=T)
    in_maps = [make_core_inputs(x[i], W) for i in range(B)]
    res = run_bass_kernel_spmd(nc, in_maps, core_ids=list(range(B)))
    out = np.stack([res.results[i]["y"] for i in range(B)], axis=0)
    return np.ascontiguousarray(out.transpose(0, 2, 1))
